# version 05
# baseline (speedup 1.0000x reference)
import numpy as np
import ml_dtypes
import concourse.bass as bass
import concourse.mybir as mybir
from concourse.bass_utils import run_bass_kernel_spmd

F32 = mybir.dt.float32
BF16 = mybir.dt.bfloat16
U8 = mybir.dt.uint8
AF = mybir.ActivationFunctionType
ALU = mybir.AluOpType
AX = mybir.AxisListType

D = 2048
NH = 16
HD = 128
SLAB = 2048
OWN = 1024
NG = 8
SCALE = 1.0 / np.sqrt(128.0)
CAP = 30.0
EPS = 1e-5

ENGS = ("tensor", "vector", "scalar", "gpsimd", "sync")


def build_nc(dump=False):
    nc = bass.Bass("TRN2", target_bir_lowering=False, debug=False)

    xs = nc.dram_tensor("xs", [SLAB, D], F32, kind="ExternalInput").ap()
    wq = nc.dram_tensor("wq", [8, 128, 16, 256], F32, kind="ExternalInput").ap()
    wk = nc.dram_tensor("wk", [8, 128, 16, 256], F32, kind="ExternalInput").ap()
    wv = nc.dram_tensor("wv", [8, 128, 16, 256], F32, kind="ExternalInput").ap()
    wf = nc.dram_tensor("wf", [4, 128, 16, 512], F32, kind="ExternalInput").ap()
    wo = nc.dram_tensor("wo", [4, 128, 16, 512], F32, kind="ExternalInput").ap()
    biasb = nc.dram_tensor("biasb", [128, D], F32, kind="ExternalInput").ap()
    vcol = nc.dram_tensor("vcol", [128, 16, 1], F32, kind="ExternalInput").ap()
    trih = nc.dram_tensor("trih", [128, 256], BF16, kind="ExternalInput").ap()
    identh = nc.dram_tensor("identh", [128, 128], BF16, kind="ExternalInput").ap()
    onesch = nc.dram_tensor("onesch", [128, 1], BF16, kind="ExternalInput").ap()
    onesrh = nc.dram_tensor("onesrh", [1, 128], F32, kind="ExternalInput").ap()
    y = nc.dram_tensor("y", [OWN, D], F32, kind="ExternalOutput").ap()
    if dump:
        dd = {
            "d_xn": [128, 16, SLAB, BF16], "d_k": [128, 2, SLAB, BF16],
            "d_v": [128, 2, SLAB, BF16], "d_q": [128, 2, OWN, BF16],
            "d_vaug": [128, 16, 288, BF16], "d_e": [128, 12, 512, BF16],
            "d_onm": [128, 4, 128, BF16], "d_ofm": [128, 16, OWN, BF16],
            "d_z": [128, 16, 1024, BF16], "d_alpha": [128, 512, F32],
        }
        dt_ = {k: nc.dram_tensor(k, v[:-1], v[-1], kind="ExternalOutput").ap()
               for k, v in dd.items()}

    xn_fm = nc.alloc_sbuf_tensor("xn_fm", [128, 16, SLAB], BF16)
    o_fm = nc.alloc_sbuf_tensor("o_fm", [128, 16, OWN], BF16)
    bias_sb = nc.alloc_sbuf_tensor("bias_sb", [128, D], F32)
    wsl = nc.alloc_sbuf_tensor("wsl", [128, 2, 2, 16, 256], BF16)
    wqsl = nc.alloc_sbuf_tensor("wqsl", [128, 16, 256], BF16)
    tri = nc.alloc_sbuf_tensor("tri", [128, 256], BF16)
    ident = nc.alloc_sbuf_tensor("ident", [128, 128], BF16)
    vcol_sb = nc.alloc_sbuf_tensor("vcol_sb", [128, 16, 1], F32)
    onesc = nc.alloc_sbuf_tensor("onesc", [128, 1], BF16)
    onesr = nc.alloc_sbuf_tensor("onesr", [1, 128], F32)
    stats = nc.alloc_sbuf_tensor("stats", [128, 2, 8], F32)
    o_nm = nc.alloc_sbuf_tensor("o_nm", [128, 4, 128], BF16)
    rcol = nc.alloc_sbuf_tensor("rcol", [128, 4, 1], F32)
    arow = nc.alloc_sbuf_tensor("arow", [1, 512], F32)

    AR = 56 * 1024
    back = nc.alloc_sbuf_tensor("arena", [128, AR], U8)
    base = nc.lookup_mloc(back).addr

    def carve(name, shape, dtype, off):
        return nc.alloc_sbuf_tensor_at(name, shape, dtype, offset=base + off)

    xbuf = carve("xbuf", [128, 3, D], F32, 0)
    xn_tm = carve("xn_tm", [128, 2, D], BF16, 24576)
    sq_scr = carve("sq_scr", [128, D], F32, 32768)
    k_fm = carve("k_fm", [128, 2, SLAB], BF16, 0)
    v_fm = carve("v_fm", [128, 2, SLAB], BF16, 8192)
    q_fm = carve("q_fm", [128, 2, OWN], BF16, 16384)
    v_aug = carve("v_aug", [128, 16, 288], BF16, 20480)
    ebuf = carve("ebuf", [128, 12, 512], BF16, 29696)
    tbuf = carve("tbuf", [128, 1536], F32, 41984)
    wfb = carve("wfb", [128, 2, 16, 512], BF16, 0)
    absw = carve("absw", [128, 16, 512], BF16, 32768)
    alpha_b = carve("alpha_b", [128, 2, 512], F32, 49152)
    tmp1 = carve("tmp1", [128, 512], F32, 53248)
    tmp_sg = carve("tmp_sg", [128, 2, 512], F32, 32768)
    out_st = carve("out_st", [128, 2, 1024], F32, 36864)
    xr = carve("xr", [128, 2, 1024], F32, 45056)

    pr_ps = nc.alloc_psum_tensor("pr_ps", [128, 1024], F32)
    s_ps = nc.alloc_psum_tensor("s_ps", [128, 1536], F32)
    o_ps = nc.alloc_psum_tensor("o_ps", [128, 1024], F32)
    t_ps = nc.alloc_psum_tensor("t_ps", [128, 512], BF16)

    plan = {e: [] for e in ENGS}
    cnt = {}
    waited = {}

    def inc(sem, by=1):
        cnt[sem] = cnt.get(sem, 0) + by
        return cnt[sem]

    def val(sem):
        return cnt.get(sem, 0)

    def emit(eng, fn):
        plan[eng].append(fn)

    def W(eng, sem, v):
        if v <= 0:
            return
        if v > waited.get((eng, sem), 0):
            waited[(eng, sem)] = v
            emit(eng, lambda e, s=sem, vv=v: e.wait_ge(e.bass._sems[s], vv))

    def op(eng, f, isem=None, iby=1):
        if isem is None:
            emit(eng, f)
            return None
        v = inc(isem, iby)
        emit(eng, lambda e, f_=f, s=isem, b=iby: f_(e).then_inc(e.bass._sems[s], b))
        return v

    op("sync", lambda e: e.dma_start(out=tri[:], in_=trih), "ct", 16)
    op("sync", lambda e: e.dma_start(out=ident[:], in_=identh), "ct", 16)
    op("sync", lambda e: e.dma_start(out=vcol_sb[:], in_=vcol), "ct", 16)
    op("sync", lambda e: e.dma_start(out=bias_sb[:], in_=biasb), "ct", 16)
    op("sync", lambda e: e.dma_start(out=onesc[:], in_=onesch), "ct", 16)
    op("sync", lambda e: e.dma_start(out=onesr[:], in_=onesrh), "ct", 16)
    w_kv = [0] * NG
    w_q = [0] * NG
    for wi, wt in ((0, wk), (1, wv)):
        op("gpsimd", lambda e, wi=wi, wt=wt: e.dma_start(out=wsl[:, 0, wi], in_=wt[0]), "w", 16)
    w_kv[0] = val("w")
    op("gpsimd", lambda e: e.dma_start(out=wqsl[:], in_=wq[0]), "w", 16)
    w_q[0] = val("w")

    ln_scale = 1.0 / float(D)
    rs_at = [0] * 16
    vs_at = [0] * 16
    xn_at = [0] * 16

    xl_at = [None] * 16

    def p0_load(tc):
        eng, sem = ("gpsimd", "xg") if tc % 2 == 0 else ("sync", "xl")
        W(eng, "xn", tc - 2)
        v = op(eng, lambda e, tc=tc: e.dma_start(
            out=xbuf[:, tc % 3], in_=xs[tc * 128:(tc + 1) * 128, :]), sem, 16)
        xl_at[tc] = (sem, v)

    def p0_square(tc):
        W("scalar", *xl_at[tc])
        op("scalar", lambda e, p=tc % 2, b=tc % 3: e.activation(
            sq_scr[:], xbuf[:, b], AF.Square, accum_out=stats[:, p, 1:2]), "sq")

    def p0_stats(tc):
        p, b = tc % 2, tc % 3
        W("vector", *xl_at[tc])
        op("vector", lambda e, p=p, b=b: e.reduce_sum(stats[:, p, 0:1], xbuf[:, b], axis=AX.X), "vv")
        W("vector", "vv", val("vv"))
        W("vector", "sq", tc + 1)
        op("vector", lambda e, p=p: e.tensor_scalar(stats[:, p, 2:3], stats[:, p, 0:1], ln_scale, None, ALU.mult))
        op("vector", lambda e, p=p: e.tensor_scalar(stats[:, p, 3:4], stats[:, p, 1:2], ln_scale, None, ALU.mult), "vv")
        W("vector", "vv", val("vv"))
        op("vector", lambda e, p=p: e.tensor_tensor(stats[:, p, 4:5], stats[:, p, 2:3], stats[:, p, 2:3], ALU.mult), "vv")
        W("vector", "vv", val("vv"))
        op("vector", lambda e, p=p: e.scalar_tensor_tensor(
            stats[:, p, 7:8], stats[:, p, 3:4], EPS, stats[:, p, 4:5], ALU.add, ALU.subtract), "vs")
        vs_at[tc] = val("vs")

    def p0_rstd(tc):
        p = tc % 2
        W("scalar", "vs", vs_at[tc])
        op("scalar", lambda e, p=p: e.activation(stats[:, p, 5:6], stats[:, p, 7:8], AF.Ln))
        op("scalar", lambda e, p=p: e.activation(stats[:, p, 6:7], stats[:, p, 5:6], AF.Exp, scale=-0.5), "rs")
        rs_at[tc] = val("rs")

    def p0_affine(tc):
        p, b = tc % 2, tc % 3
        W("vector", "rs", rs_at[tc])
        if tc >= 2:
            W("vector", *tp_at[tc - 2])
        op("vector", lambda e, p=p, b=b: e.tensor_scalar(
            xn_tm[:, p], xbuf[:, b], stats[:, p, 2:3], stats[:, p, 6:7],
            ALU.subtract, ALU.mult), "xn")
        xn_at[tc] = val("xn")

    tp_at = [None] * 16

    def p0_tp(tc):
        eng, sem = ("scalar", "tp") if tc % 2 == 0 else ("sync", "tq")
        W(eng, "xn", xn_at[tc])
        v = op(eng, lambda e, tc=tc, p=tc % 2: e.dma_start(
            out=xn_fm[:, :, tc * 128:(tc + 1) * 128], in_=xn_tm[:, p], transpose=True), sem, 16)
        tp_at[tc] = (sem, v)

    for t in range(17):
        if t < 16:
            p0_load(t)
            p0_square(t)
        if 1 <= t:
            p0_rstd(t - 1)
        if t < 16:
            p0_stats(t)
        if 1 <= t:
            p0_affine(t - 1)
            p0_tp(t - 1)

    pr_hist = [0, 0]
    vt_prev = 0
    s_hist = 0
    o_hist = [0, 0]
    t_hist = 0
    e_hist = 0
    kvq_done = [0] * NG
    prtile = 0

    def proj_tile(dest_ap, wslice, rhs_ap):
        nonlocal prtile
        pb = prtile % 2
        prtile += 1
        W("tensor", "ev", pr_hist[pb])
        for dk in range(16):
            op("tensor", lambda e, pb=pb, dk=dk, w=wslice, r=rhs_ap:
               e.matmul(pr_ps[:, pb * 512:pb * 512 + 512], w(dk), r(dk),
                        start=(dk == 0), stop=(dk == 15)),
               "mm" if dk == 15 else None)
        m = val("mm")
        W("vector", "mm", m)
        op("vector", lambda e, pb=pb, d=dest_ap: e.tensor_copy(d, pr_ps[:, pb * 512:pb * 512 + 512]), "ev")
        pr_hist[pb] = val("ev")

    W("tensor", "tp", 16 * 8)
    W("tensor", "tq", 16 * 8)
    W("tensor", "ct", 16 * 6)

    W("vector", "ct", 16 * 6)
    W("vector", "sq", 16)
    W("vector", "tp", 16 * 8)
    W("vector", "tq", 16 * 8)
    for hh in range(2):
        op("vector", lambda e, hh=hh: e.tensor_copy(
            v_aug[:, :, hh * 144 + 128:hh * 144 + 129], vcol_sb[:]), "ev")

    if dump:
        W("sync", "tp", 16 * 8)
        W("sync", "tq", 16 * 8)
        op("sync", lambda e: e.dma_start(out=dt_["d_xn"], in_=xn_fm[:]), "st", 16)

    for g in range(NG):
        gb = g % 2
        if g + 1 < NG:
            W("gpsimd", "mm", kvq_done[g - 1] if g >= 1 else 0)
            for wi, wt in ((0, wk), (1, wv)):
                op("gpsimd", lambda e, wi=wi, wt=wt, g=g: e.dma_start(
                    out=wsl[:, (g + 1) % 2, wi], in_=wt[g + 1]), "w", 16)
            w_kv[g + 1] = val("w")
        W("tensor", "w", w_kv[g])
        W("vector", "vt", vt_prev)
        for oc in range(2):
            for nb in range(4):
                proj_tile(k_fm[:, oc, nb * 512:nb * 512 + 512],
                          lambda dk, oc=oc, gb=gb: wsl[:, gb, 0, dk, oc * 128:oc * 128 + 128],
                          lambda dk, nb=nb: xn_fm[:, dk, nb * 512:nb * 512 + 512])
        for oc in range(2):
            for nb in range(4):
                proj_tile(v_fm[:, oc, nb * 512:nb * 512 + 512],
                          lambda dk, oc=oc, gb=gb: wsl[:, gb, 1, dk, oc * 128:oc * 128 + 128],
                          lambda dk, nb=nb: xn_fm[:, dk, nb * 512:nb * 512 + 512])
            W("sync", "ev", val("ev"))
            W("sync", "mm", e_hist)
            op("sync", lambda e, hh=oc: e.dma_start(
                out=v_aug[:, :, hh * 144:hh * 144 + 128], in_=v_fm[:, hh, :],
                transpose=True), "vt", 16)
        W("tensor", "w", w_q[g])
        for oc in range(2):
            for nb in range(2):
                proj_tile(q_fm[:, oc, nb * 512:nb * 512 + 512],
                          lambda dk, oc=oc: wqsl[:, dk, oc * 128:oc * 128 + 128],
                          lambda dk, nb=nb: xn_fm[:, dk, 1024 + nb * 512:1024 + nb * 512 + 512])
        kvq_done[g] = val("mm")
        if g + 1 < NG:
            W("gpsimd", "mm", kvq_done[g])
            op("gpsimd", lambda e, g=g: e.dma_start(out=wqsl[:], in_=wq[g + 1]), "w", 16)
            w_q[g + 1] = val("w")
        vt_grp = val("vt")
        vt_prev = vt_grp
        if dump and g == 0:
            W("sync", "ev", val("ev"))
            W("sync", "vt", val("vt"))
            op("sync", lambda e: e.dma_start(out=dt_["d_k"], in_=k_fm[:]), "st", 16)
            op("sync", lambda e: e.dma_start(out=dt_["d_v"], in_=v_fm[:]), "st", 16)
            op("sync", lambda e: e.dma_start(out=dt_["d_q"], in_=q_fm[:]), "st", 16)
            op("sync", lambda e: e.dma_start(out=dt_["d_vaug"], in_=v_aug[:]), "st", 16)
            W("tensor", "st", val("st"))

        for h in range(2):
            ha = 2 * g + h
            for qB in range(2):
                lq0 = 4 * qB
                for b in range(4):
                    W("tensor", "ac", s_hist)
                    for e3 in range(3):
                        kc = lq0 + 3 * b + e3
                        op("tensor", lambda e, e3=e3, kc=kc, h=h, qB=qB: e.matmul(
                            s_ps[:, e3 * 512:e3 * 512 + 512],
                            k_fm[:, h, kc * 128:kc * 128 + 128],
                            q_fm[:, h, qB * 512:qB * 512 + 512],
                            start=True, stop=True),
                           "mm" if e3 == 2 else None)
                    W("scalar", "mm", val("mm"))
                    if b == 0:
                        W("scalar", "mm", e_hist)
                    op("scalar", lambda e: e.activation(tbuf[:], s_ps[:, 0:1536], AF.Tanh,
                                                        scale=float(SCALE / CAP)), "ac")
                    s_hist = val("ac")
                    op("scalar", lambda e, b=b: e.activation(ebuf[:, 3 * b:3 * b + 3, :], tbuf[:],
                                                             AF.Exp, scale=float(CAP)), "ac")
                W("vector", "ac", val("ac"))
                for j in range(4):
                    op("vector", lambda e, j=j: e.tensor_tensor(
                        ebuf[:, j, j * 128:j * 128 + 128], ebuf[:, j, j * 128:j * 128 + 128],
                        tri[:, 0:128], ALU.mult))
                    op("vector", lambda e, j=j: e.tensor_tensor(
                        ebuf[:, j + 8, j * 128:j * 128 + 128], ebuf[:, j + 8, j * 128:j * 128 + 128],
                        tri[:, 128:256], ALU.mult), "ev" if j == 3 else None)
                msk_done = val("ev")
                if dump and g == 0 and h == 0 and qB == 0:
                    W("sync", "ev", msk_done)
                    op("sync", lambda e: e.dma_start(out=dt_["d_e"], in_=ebuf[:]), "st", 16)
                    W("tensor", "st", val("st"))
                W("tensor", "vt", vt_grp)
                W("tensor", "ev", msk_done)
                for pair in range(2):
                    W("tensor", "ev", o_hist[pair])
                    for j in (2 * pair, 2 * pair + 1):
                        off = pair * 512 + (j % 2) * 256
                        for m in range(9):
                            kc = lq0 + j + m
                            op("tensor", lambda e, j=j, m=m, kc=kc, h=h, off=off: e.matmul(
                                o_ps[:, off:off + 129],
                                ebuf[:, j + m, j * 128:j * 128 + 128],
                                v_aug[:, kc, h * 144:h * 144 + 129],
                                start=(m == 0), stop=(m == 8)),
                               "mm" if (m == 8 and j % 2 == 1) else None)
                    m_pv = val("mm")
                    W("vector", "mm", m_pv)
                    for j in (2 * pair, 2 * pair + 1):
                        off = pair * 512 + (j % 2) * 256
                        op("vector", lambda e, j=j, off=off: e.reciprocal(
                            rcol[:, j], o_ps[:, off + 128:off + 129]), "vv" if j % 2 == 1 else None)
                    W("vector", "vv", val("vv"))
                    for j in (2 * pair, 2 * pair + 1):
                        off = pair * 512 + (j % 2) * 256
                        op("vector", lambda e, j=j, off=off: e.tensor_scalar(
                            o_nm[:, j, :], o_ps[:, off:off + 128], rcol[:, j], None, ALU.mult),
                           "ev" if j % 2 == 1 else None)
                    o_hist[pair] = val("ev")
                e_hist = val("mm")
                if dump and g == 0 and h == 0 and qB == 0:
                    W("sync", "ev", val("ev"))
                    op("sync", lambda e: e.dma_start(out=dt_["d_onm"], in_=o_nm[:]), "st", 16)
                    W("vector", "st", val("st"))
                W("tensor", "ev", val("ev"))
                W("tensor", "ev", t_hist)
                for j in range(4):
                    op("tensor", lambda e, j=j: e.transpose(
                        t_ps[:, j * 128:j * 128 + 128], o_nm[:, j, :], ident[:]),
                       "mm" if j == 3 else None)
                W("vector", "mm", val("mm"))
                op("vector", lambda e, ha=ha, qB=qB: e.tensor_copy(
                    o_fm[:, ha, qB * 512:qB * 512 + 512], t_ps[:]), "ev")
                t_hist = val("ev")

    pa_mm_done = val("mm")
    pa_ev_done = val("ev")
    if dump:
        W("sync", "ev", pa_ev_done)
        op("sync", lambda e: e.dma_start(out=dt_["d_ofm"], in_=o_fm[:]), "st", 16)

    def zview(tc, ob):
        return xn_fm[:, 2 * tc + ob // 2, (ob % 2) * 512:(ob % 2) * 512 + 512]

    wfb_hist = [pa_mm_done, pa_mm_done]
    silu_ac = [[0] * 4 for _ in range(8)]
    absw_pe = 0
    sgn_at = [0] * 4
    aev_at = [0] * 4

    def c1_load(ob):
        W("gpsimd", "mm", wfb_hist[ob % 2])
        op("gpsimd", lambda e, ob=ob: e.dma_start(out=wfb[:, ob % 2], in_=wf[ob]), "w", 16)

    def c1_alpha(ob):
        nonlocal absw_pe
        fb = ob % 2
        W("vector", "w", val("w"))
        if ob == 0:
            W("vector", "mm", pa_mm_done)
        W("vector", "mm", absw_pe)
        op("vector", lambda e, fb=fb: e.scalar_tensor_tensor(
            absw[:].rearrange("p a b -> p (a b)"), wfb[:, fb].rearrange("p a b -> p (a b)"),
            -1.0, wfb[:, fb].rearrange("p a b -> p (a b)"), ALU.mult, ALU.max), "ev")
        absw_done = val("ev")
        op("vector", lambda e, fb=fb: e.tensor_scalar(
            wfb[:, fb].rearrange("p a b -> p (a b)"), wfb[:, fb].rearrange("p a b -> p (a b)"),
            0.0, 0.5, ALU.is_ge, ALU.subtract), "ev")
        sgn_at[ob] = val("ev")
        W("tensor", "ev", absw_done)
        W("tensor", "ac", s_hist)
        for dk in range(16):
            op("tensor", lambda e, dk=dk: e.matmul(s_ps[0:1, 0:512], onesc[:], absw[:, dk, :],
                                                   start=(dk == 0), stop=(dk == 15)),
               "mm" if dk == 15 else None)
        absw_pe = val("mm")
        W("vector", "mm", val("mm"))
        op("vector", lambda e: e.tensor_copy(arow[:], s_ps[0:1, 0:512]), "ev")
        W("tensor", "ev", val("ev"))
        op("tensor", lambda e: e.matmul(s_ps[:, 512:1024], onesr[:], arow[:], start=True, stop=True), "mm")
        W("vector", "mm", val("mm"))
        op("vector", lambda e, fb=fb: e.tensor_copy(alpha_b[:, fb], s_ps[:, 512:1024]), "ev")
        aev_at[ob] = val("ev")

    def c1_ffn(ob):
        nonlocal prtile
        fb = ob % 2
        W("tensor", "ev", sgn_at[ob])
        for tc in range(8):
            pb = prtile % 2
            prtile += 1
            W("tensor", "ev", pr_hist[pb])
            for dk in range(16):
                op("tensor", lambda e, pb=pb, dk=dk, tc=tc, fb=fb: e.matmul(
                    pr_ps[:, pb * 512:pb * 512 + 512],
                    xn_fm[:, dk, 1024 + tc * 128:1024 + tc * 128 + 128],
                    wfb[:, fb, dk, :], start=(dk == 0), stop=(dk == 15)),
                   "mm" if dk == 15 else None)
            m = val("mm")
            W("vector", "mm", m)
            W("vector", "ev", aev_at[ob])
            op("vector", lambda e, pb=pb, fb=fb: e.scalar_tensor_tensor(
                tmp1[:], pr_ps[:, pb * 512:pb * 512 + 512], 1.0, alpha_b[:, fb], ALU.mult, ALU.mult))
            op("vector", lambda e, tc=tc, ob=ob: e.tensor_tensor(
                zview(tc, ob), tmp1[:], bias_sb[:, ob * 512:ob * 512 + 512], ALU.add), "ev")
            pr_hist[pb] = val("ev")
            W("scalar", "ev", val("ev"))
            op("scalar", lambda e, tc=tc, ob=ob: e.activation(
                zview(tc, ob), zview(tc, ob), AF.Silu), "ac")
            silu_ac[tc][ob] = val("ac")
        wfb_hist[fb] = val("mm")

    c1_load(0)
    c1_load(1)
    c1_alpha(0)
    c1_alpha(1)
    c1_ffn(0)
    c1_load(2)
    c1_ffn(1)
    c1_alpha(2)
    c1_load(3)
    c1_ffn(2)
    c1_alpha(3)
    c1_ffn(3)
    if dump:
        W("sync", "ev", aev_at[0] if False else val("ev"))
        op("sync", lambda e: e.dma_start(out=dt_["d_alpha"], in_=alpha_b[:, 1]), "st", 16)
    if dump:
        W("sync", "ac", silu_ac[7][3])
        op("sync", lambda e: e.dma_start(out=dt_["d_z"], in_=xn_fm[:, :, 0:1024]), "st", 16)
        W("scalar", "st", val("st"))

    xr_hist = [0, 0]
    st_hist = [0, 0]
    for half in range(2):
        for j2 in range(2):
            ob = half * 2 + j2
            W("gpsimd", "mm", wfb_hist[ob % 2])
            op("gpsimd", lambda e, ob=ob: e.dma_start(out=wfb[:, ob % 2], in_=wo[ob]), "w", 16)
        W("tensor", "w", val("w"))
        for tc in range(8):
            slot = tc % 2
            W("scalar", "ev", xr_hist[slot])
            op("scalar", lambda e, tc=tc, half=half, slot=slot: e.dma_start(
                out=xr[:, slot], in_=xs[1024 + tc * 128:1024 + tc * 128 + 128,
                                        half * 1024:half * 1024 + 1024]), "xr", 16)
            for j2 in range(2):
                ob = half * 2 + j2
                pb = prtile % 2
                prtile += 1
                W("tensor", "ev", pr_hist[pb])
                for dk in range(16):
                    op("tensor", lambda e, pb=pb, dk=dk, tc=tc, fb=ob % 2: e.matmul(
                        pr_ps[:, pb * 512:pb * 512 + 512],
                        o_fm[:, dk, tc * 128:tc * 128 + 128],
                        wfb[:, fb, dk, :], start=(dk == 0), stop=(dk == 15)),
                       "mm" if dk == 15 else None)
                m = val("mm")
                W("vector", "mm", m)
                W("vector", "ac", silu_ac[tc][ob])
                W("vector", "xr", val("xr"))
                W("vector", "st", st_hist[slot])
                op("vector", lambda e, pb=pb, tc=tc, ob=ob, j2=j2: e.scalar_tensor_tensor(
                    tmp_sg[:, j2], pr_ps[:, pb * 512:pb * 512 + 512], 1.0, zview(tc, ob),
                    ALU.mult, ALU.mult))
                op("vector", lambda e, slot=slot, j2=j2: e.tensor_tensor(
                    out_st[:, slot, j2 * 512:j2 * 512 + 512], tmp_sg[:, j2],
                    xr[:, slot, j2 * 512:j2 * 512 + 512], ALU.add), "ev")
                pr_hist[pb] = val("ev")
            xr_hist[slot] = val("ev")
            W("sync", "ev", val("ev"))
            op("sync", lambda e, tc=tc, half=half, slot=slot: e.dma_start(
                out=y[tc * 128:tc * 128 + 128, half * 1024:half * 1024 + 1024],
                in_=out_st[:, slot]), "st", 16)
            st_hist[slot] = val("st")
        wfb_hist[0] = val("mm")
        wfb_hist[1] = val("mm")

    W("sync", "st", val("st"))

    import contextlib
    with contextlib.ExitStack() as stack:
        sems = {}
        for name in cnt:
            sems[name] = stack.enter_context(nc.semaphore(name))
        nc._sems = sems
        with nc.Block() as blk:
            @blk.tensor
            def _(e):
                for f in plan["tensor"]:
                    f(e)

            @blk.vector
            def _(e):
                for f in plan["vector"]:
                    f(e)

            @blk.scalar
            def _(e):
                for f in plan["scalar"]:
                    f(e)

            @blk.gpsimd
            def _(e):
                for f in plan["gpsimd"]:
                    f(e)

            @blk.sync
            def _(e):
                for f in plan["sync"]:
                    f(e)
    return nc


def _prep_in_maps(L0_attn_x, L0_attn_Wq, L0_attn_Wk, L0_attn_Wv, L0_attn_Wo,
                  L0_ffn_weight, L0_ffn_bias):
    x = np.ascontiguousarray(np.asarray(L0_attn_x, dtype=np.float32))[0]

    def blk(w, ncols):
        wT = np.asarray(w, dtype=np.float32).T
        ng = 2048 // ncols
        return np.ascontiguousarray(wT.reshape(16, 128, ng, ncols).transpose(2, 1, 0, 3))

    wq_b = blk(L0_attn_Wq, 256)
    wk_b = blk(L0_attn_Wk, 256)
    wv_b = blk(L0_attn_Wv, 256)
    wo_b = blk(L0_attn_Wo, 512)
    wf_b = blk(L0_ffn_weight, 512)
    biasb = np.ascontiguousarray(np.broadcast_to(
        np.asarray(L0_ffn_bias, dtype=np.float32)[None, :], (128, 2048)))
    kk, qq = np.meshgrid(np.arange(128), np.arange(128), indexing="ij")
    tri = np.concatenate([(qq < kk), (qq >= kk)], axis=1).astype(ml_dtypes.bfloat16)
    ident = np.eye(128, dtype=ml_dtypes.bfloat16)
    onesc = np.full((128, 1), 2.0 / 2048.0, dtype=ml_dtypes.bfloat16)
    onesr = np.ones((1, 128), dtype=np.float32)

    in_maps = []
    for c in range(8):
        if c == 0:
            xsc = np.concatenate([np.zeros((1024, 2048), np.float32), x[0:1024]], axis=0)
            valid = np.array([0.0] * 8 + [1.0] * 8, np.float32)
        else:
            xsc = x[c * 1024 - 1024: c * 1024 + 1024]
            valid = np.ones(16, np.float32)
        xsc = np.ascontiguousarray(xsc)
        vc = np.ascontiguousarray(np.broadcast_to(valid[None, :, None], (128, 16, 1)).astype(np.float32))
        in_maps.append({
            "xs": xsc, "wq": wq_b, "wk": wk_b, "wv": wv_b, "wf": wf_b, "wo": wo_b,
            "biasb": biasb, "vcol": vc, "trih": tri, "identh": ident,
            "onesch": onesc, "onesrh": onesr,
        })
    return in_maps


def kernel(**inputs):
    in_maps = _prep_in_maps(**inputs)
    nc = build_nc()
    res = run_bass_kernel_spmd(nc, in_maps, core_ids=list(range(8)))
    out = np.concatenate([res.results[c]["y"] for c in range(8)], axis=0)
    return out[None, :, :].astype(np.float32)


if __name__ == "__main__":
    rng = np.random.RandomState(0)
    ins = {
        "L0_attn_x": rng.randn(1, 8192, 2048).astype(np.float32),
        "L0_attn_Wq": (rng.randn(2048, 2048) / 45.25).astype(np.float32),
        "L0_attn_Wk": (rng.randn(2048, 2048) / 45.25).astype(np.float32),
        "L0_attn_Wv": (rng.randn(2048, 2048) / 45.25).astype(np.float32),
        "L0_attn_Wo": (rng.randn(2048, 2048) / 45.25).astype(np.float32),
        "L0_ffn_weight": (rng.randn(2048, 2048) * 0.03125).astype(np.float32),
        "L0_ffn_bias": (rng.randn(2048) * 0.01).astype(np.float32),
    }
    out = kernel(**ins)
    print(out.shape, out.dtype, np.abs(out).mean())


# version 06
# speedup vs baseline: 1.2205x; 1.2205x over previous
import numpy as np
import ml_dtypes
import concourse.bass as bass
import concourse.mybir as mybir
from concourse.bass_utils import run_bass_kernel_spmd

F32 = mybir.dt.float32
BF16 = mybir.dt.bfloat16
U8 = mybir.dt.uint8
AF = mybir.ActivationFunctionType
ALU = mybir.AluOpType
AX = mybir.AxisListType

D = 2048
NH = 16
HD = 128
SLAB = 2048
OWN = 1024
NG = 8
SCALE = 1.0 / np.sqrt(128.0)
CAP = 30.0
EPS = 1e-5

ENGS = ("tensor", "vector", "scalar", "gpsimd", "sync")


def build_nc(dump=False):
    nc = bass.Bass("TRN2", target_bir_lowering=False, debug=False)

    xs = nc.dram_tensor("xs", [SLAB, D], F32, kind="ExternalInput").ap()
    wq = nc.dram_tensor("wq", [8, 128, 16, 256], F32, kind="ExternalInput").ap()
    wk = nc.dram_tensor("wk", [8, 128, 16, 256], F32, kind="ExternalInput").ap()
    wv = nc.dram_tensor("wv", [8, 128, 16, 256], F32, kind="ExternalInput").ap()
    wf = nc.dram_tensor("wf", [4, 128, 16, 512], F32, kind="ExternalInput").ap()
    wo = nc.dram_tensor("wo", [4, 128, 16, 512], F32, kind="ExternalInput").ap()
    biasb = nc.dram_tensor("biasb", [128, D], F32, kind="ExternalInput").ap()
    vcol = nc.dram_tensor("vcol", [128, 16, 1], F32, kind="ExternalInput").ap()
    trih = nc.dram_tensor("trih", [128, 256], BF16, kind="ExternalInput").ap()
    identh = nc.dram_tensor("identh", [128, 128], BF16, kind="ExternalInput").ap()
    onesch = nc.dram_tensor("onesch", [128, 1], BF16, kind="ExternalInput").ap()
    onesrh = nc.dram_tensor("onesrh", [1, 128], F32, kind="ExternalInput").ap()
    y = nc.dram_tensor("y", [OWN, D], F32, kind="ExternalOutput").ap()
    if dump:
        dd = {
            "d_xn": [128, 16, SLAB, BF16], "d_k": [128, 2, SLAB, BF16],
            "d_v": [128, 2, SLAB, BF16], "d_q": [128, 2, OWN, BF16],
            "d_vaug": [128, 16, 288, BF16], "d_e": [128, 12, 512, BF16],
            "d_onm": [128, 4, 128, BF16], "d_ofm": [128, 16, OWN, BF16],
            "d_z": [128, 16, 1024, BF16], "d_alpha": [128, 512, F32],
        }
        dt_ = {k: nc.dram_tensor(k, v[:-1], v[-1], kind="ExternalOutput").ap()
               for k, v in dd.items()}

    xn_fm = nc.alloc_sbuf_tensor("xn_fm", [128, 16, SLAB], BF16)
    o_fm = nc.alloc_sbuf_tensor("o_fm", [128, 16, OWN], BF16)
    bias_sb = nc.alloc_sbuf_tensor("bias_sb", [128, D], F32)
    wsl = nc.alloc_sbuf_tensor("wsl", [128, 2, 2, 16, 256], BF16)
    wqsl = nc.alloc_sbuf_tensor("wqsl", [128, 16, 256], BF16)
    tri = nc.alloc_sbuf_tensor("tri", [128, 256], BF16)
    ident = nc.alloc_sbuf_tensor("ident", [128, 128], BF16)
    vcol_sb = nc.alloc_sbuf_tensor("vcol_sb", [128, 16, 1], F32)
    onesc = nc.alloc_sbuf_tensor("onesc", [128, 1], BF16)
    onesr = nc.alloc_sbuf_tensor("onesr", [1, 128], F32)
    stats = nc.alloc_sbuf_tensor("stats", [128, 4, 8], F32)
    o_nm = nc.alloc_sbuf_tensor("o_nm", [128, 4, 128], BF16)
    rcol = nc.alloc_sbuf_tensor("rcol", [128, 4, 1], F32)
    arow = nc.alloc_sbuf_tensor("arow", [1, 512], F32)

    AR = 56 * 1024
    back = nc.alloc_sbuf_tensor("arena", [128, AR], U8)
    base = nc.lookup_mloc(back).addr

    def carve(name, shape, dtype, off):
        return nc.alloc_sbuf_tensor_at(name, shape, dtype, offset=base + off)

    xbuf = carve("xbuf", [128, 5, D], F32, 0)
    xn_tm = carve("xn_tm", [128, 2, D], BF16, 40960)
    sq_scr = carve("sq_scr", [128, D], F32, 49152)
    k_fm = carve("k_fm", [128, 2, SLAB], BF16, 0)
    v_fm = carve("v_fm", [128, 2, SLAB], BF16, 8192)
    q_fm = carve("q_fm", [128, 2, OWN], BF16, 16384)
    v_aug = carve("v_aug", [128, 16, 288], BF16, 20480)
    ebuf = carve("ebuf", [128, 12, 512], BF16, 29696)
    tbuf = carve("tbuf", [128, 1536], F32, 41984)
    wfb = carve("wfb", [128, 2, 16, 512], BF16, 0)
    absw = carve("absw", [128, 16, 512], BF16, 32768)
    alpha_b = carve("alpha_b", [128, 2, 512], F32, 49152)
    tmp1 = carve("tmp1", [128, 512], F32, 53248)
    tmp_sg = carve("tmp_sg", [128, 2, 512], F32, 32768)
    out_st = carve("out_st", [128, 2, 1024], F32, 36864)
    xr = carve("xr", [128, 2, 1024], F32, 45056)

    pr_ps = nc.alloc_psum_tensor("pr_ps", [128, 1024], F32)
    s_ps = nc.alloc_psum_tensor("s_ps", [128, 1536], F32)
    o_ps = nc.alloc_psum_tensor("o_ps", [128, 1024], F32)
    t_ps = nc.alloc_psum_tensor("t_ps", [128, 512], BF16)

    plan = {e: [] for e in ENGS}
    cnt = {}
    waited = {}

    def inc(sem, by=1):
        cnt[sem] = cnt.get(sem, 0) + by
        return cnt[sem]

    def val(sem):
        return cnt.get(sem, 0)

    def emit(eng, fn):
        plan[eng].append(fn)

    def W(eng, sem, v):
        if v <= 0:
            return
        if v > waited.get((eng, sem), 0):
            waited[(eng, sem)] = v
            emit(eng, lambda e, s=sem, vv=v: e.wait_ge(e.bass._sems[s], vv))

    def op(eng, f, isem=None, iby=1):
        if isem is None:
            emit(eng, f)
            return None
        v = inc(isem, iby)
        emit(eng, lambda e, f_=f, s=isem, b=iby: f_(e).then_inc(e.bass._sems[s], b))
        return v

    op("sync", lambda e: e.dma_start(out=tri[:], in_=trih), "ct", 16)
    op("sync", lambda e: e.dma_start(out=ident[:], in_=identh), "ct", 16)
    op("sync", lambda e: e.dma_start(out=vcol_sb[:], in_=vcol), "ct", 16)
    op("sync", lambda e: e.dma_start(out=bias_sb[:], in_=biasb), "ct", 16)
    op("sync", lambda e: e.dma_start(out=onesc[:], in_=onesch), "ct", 16)
    op("sync", lambda e: e.dma_start(out=onesr[:], in_=onesrh), "ct", 16)
    w_kv = [0] * NG
    w_q = [0] * NG

    ln_scale = 1.0 / float(D)
    rs_at = [0] * 16
    vs_at = [0] * 16
    xn_at = [0] * 16

    xl_at = [None] * 16

    def p0_load(tc):
        eng, sem = ("gpsimd", "xg") if tc % 2 == 0 else ("sync", "xl")
        W(eng, "xn", tc - 4)
        v = op(eng, lambda e, tc=tc: e.dma_start(
            out=xbuf[:, tc % 5], in_=xs[tc * 128:(tc + 1) * 128, :]), sem, 16)
        xl_at[tc] = (sem, v)

    def p0_square(tc):
        W("scalar", *xl_at[tc])
        op("scalar", lambda e, p=tc % 4, b=tc % 5: e.activation(
            sq_scr[:], xbuf[:, b], AF.Square, accum_out=stats[:, p, 1:2]), "sq")

    def p0_stats(tc):
        p, b = tc % 4, tc % 5
        W("vector", *xl_at[tc])
        op("vector", lambda e, p=p, b=b: e.reduce_sum(stats[:, p, 0:1], xbuf[:, b], axis=AX.X), "vv")
        W("vector", "vv", val("vv"))
        W("vector", "sq", tc + 1)
        op("vector", lambda e, p=p: e.tensor_scalar(stats[:, p, 2:3], stats[:, p, 0:1], ln_scale, None, ALU.mult))
        op("vector", lambda e, p=p: e.tensor_scalar(stats[:, p, 3:4], stats[:, p, 1:2], ln_scale, None, ALU.mult), "vv")
        W("vector", "vv", val("vv"))
        op("vector", lambda e, p=p: e.tensor_tensor(stats[:, p, 4:5], stats[:, p, 2:3], stats[:, p, 2:3], ALU.mult), "vv")
        W("vector", "vv", val("vv"))
        op("vector", lambda e, p=p: e.scalar_tensor_tensor(
            stats[:, p, 7:8], stats[:, p, 3:4], EPS, stats[:, p, 4:5], ALU.add, ALU.subtract), "vs")
        vs_at[tc] = val("vs")

    def p0_rstd(tc):
        p = tc % 4
        W("scalar", "vs", vs_at[tc])
        op("scalar", lambda e, p=p: e.activation(stats[:, p, 5:6], stats[:, p, 7:8], AF.Ln))
        op("scalar", lambda e, p=p: e.activation(stats[:, p, 6:7], stats[:, p, 5:6], AF.Exp, scale=-0.5), "rs")
        rs_at[tc] = val("rs")

    def p0_affine(tc):
        p, b, p2 = tc % 4, tc % 5, tc % 2
        W("vector", "rs", rs_at[tc])
        if tc >= 2:
            W("vector", *tp_at[tc - 2])
        op("vector", lambda e, p=p, b=b, p2=p2: e.tensor_scalar(
            xn_tm[:, p2], xbuf[:, b], stats[:, p, 2:3], stats[:, p, 6:7],
            ALU.subtract, ALU.mult), "xn")
        xn_at[tc] = val("xn")

    tp_at = [None] * 16

    def p0_tp(tc):
        eng, sem = ("scalar", "tp") if tc % 2 == 0 else ("sync", "tq")
        W(eng, "xn", xn_at[tc])
        v = op(eng, lambda e, tc=tc, p=tc % 2: e.dma_start(
            out=xn_fm[:, :, tc * 128:(tc + 1) * 128], in_=xn_tm[:, p], transpose=True), sem, 16)
        tp_at[tc] = (sem, v)

    for t in range(19):
        if t < 16:
            p0_load(t)
        if t == 5:
            for wi, wt in ((0, wk), (1, wv)):
                op("gpsimd", lambda e, wi=wi, wt=wt: e.dma_start(out=wsl[:, 0, wi], in_=wt[0]), "w", 16)
            w_kv[0] = val("w")
        if t == 9:
            op("gpsimd", lambda e: e.dma_start(out=wqsl[:], in_=wq[0]), "w", 16)
            w_q[0] = val("w")
        if 2 <= t < 18:
            p0_square(t - 2)
            p0_stats(t - 2)
        if 3 <= t:
            p0_rstd(t - 3)
            p0_affine(t - 3)
            p0_tp(t - 3)

    pr_hist = [0, 0]
    vt_prev = 0
    s_hist = 0
    o_hist = [0, 0]
    t_hist = 0
    e_hist = 0
    kvq_done = [0] * NG
    prtile = 0

    def proj_tile(dest_ap, wslice, rhs_ap):
        nonlocal prtile
        pb = prtile % 2
        prtile += 1
        W("tensor", "ev", pr_hist[pb])
        for dk in range(16):
            op("tensor", lambda e, pb=pb, dk=dk, w=wslice, r=rhs_ap:
               e.matmul(pr_ps[:, pb * 512:pb * 512 + 512], w(dk), r(dk),
                        start=(dk == 0), stop=(dk == 15)),
               "mm" if dk == 15 else None)
        m = val("mm")
        W("vector", "mm", m)
        op("vector", lambda e, pb=pb, d=dest_ap: e.tensor_copy(d, pr_ps[:, pb * 512:pb * 512 + 512]), "ev")
        pr_hist[pb] = val("ev")

    W("tensor", "ct", 16 * 6)

    W("vector", "ct", 16 * 6)
    W("vector", "sq", 16)
    W("vector", "tp", 16 * 8)
    W("vector", "tq", 16 * 8)
    for hh in range(2):
        op("vector", lambda e, hh=hh: e.tensor_copy(
            v_aug[:, :, hh * 144 + 128:hh * 144 + 129], vcol_sb[:]), "ev")

    if dump:
        W("sync", "tp", 16 * 8)
        W("sync", "tq", 16 * 8)
        op("sync", lambda e: e.dma_start(out=dt_["d_xn"], in_=xn_fm[:]), "st", 16)

    for g in range(NG):
        gb = g % 2
        if g + 1 < NG:
            W("gpsimd", "mm", kvq_done[g - 1] if g >= 1 else 0)
            for wi, wt in ((0, wk), (1, wv)):
                op("gpsimd", lambda e, wi=wi, wt=wt, g=g: e.dma_start(
                    out=wsl[:, (g + 1) % 2, wi], in_=wt[g + 1]), "w", 16)
            w_kv[g + 1] = val("w")
        W("tensor", "w", w_kv[g])
        W("vector", "vt", vt_prev)
        for nb in range(4):
            if g == 0:
                for tc in range(4 * nb, 4 * nb + 4):
                    W("tensor", *tp_at[tc])
            for wi, dest in ((0, k_fm), (1, v_fm)):
                for oc in range(2):
                    proj_tile(dest[:, oc, nb * 512:nb * 512 + 512],
                              lambda dk, oc=oc, gb=gb, wi=wi: wsl[:, gb, wi, dk, oc * 128:oc * 128 + 128],
                              lambda dk, nb=nb: xn_fm[:, dk, nb * 512:nb * 512 + 512])
        for hh in range(2):
            W("sync", "ev", val("ev"))
            W("sync", "mm", e_hist)
            op("sync", lambda e, hh=hh: e.dma_start(
                out=v_aug[:, :, hh * 144:hh * 144 + 128], in_=v_fm[:, hh, :],
                transpose=True), "vt", 16)
        W("tensor", "w", w_q[g])
        for nb in range(2):
            if g == 0:
                for tc in range(8 + 4 * nb, 8 + 4 * nb + 4):
                    W("tensor", *tp_at[tc])
            for oc in range(2):
                proj_tile(q_fm[:, oc, nb * 512:nb * 512 + 512],
                          lambda dk, oc=oc: wqsl[:, dk, oc * 128:oc * 128 + 128],
                          lambda dk, nb=nb: xn_fm[:, dk, 1024 + nb * 512:1024 + nb * 512 + 512])
        kvq_done[g] = val("mm")
        if g + 1 < NG:
            W("gpsimd", "mm", kvq_done[g])
            op("gpsimd", lambda e, g=g: e.dma_start(out=wqsl[:], in_=wq[g + 1]), "w", 16)
            w_q[g + 1] = val("w")
        vt_grp = val("vt")
        vt_prev = vt_grp
        if dump and g == 0:
            W("sync", "ev", val("ev"))
            W("sync", "vt", val("vt"))
            op("sync", lambda e: e.dma_start(out=dt_["d_k"], in_=k_fm[:]), "st", 16)
            op("sync", lambda e: e.dma_start(out=dt_["d_v"], in_=v_fm[:]), "st", 16)
            op("sync", lambda e: e.dma_start(out=dt_["d_q"], in_=q_fm[:]), "st", 16)
            op("sync", lambda e: e.dma_start(out=dt_["d_vaug"], in_=v_aug[:]), "st", 16)
            W("tensor", "st", val("st"))

        for h in range(2):
            ha = 2 * g + h
            for qB in range(2):
                lq0 = 4 * qB
                for b in range(4):
                    W("tensor", "ac", s_hist)
                    for e3 in range(3):
                        kc = lq0 + 3 * b + e3
                        op("tensor", lambda e, e3=e3, kc=kc, h=h, qB=qB: e.matmul(
                            s_ps[:, e3 * 512:e3 * 512 + 512],
                            k_fm[:, h, kc * 128:kc * 128 + 128],
                            q_fm[:, h, qB * 512:qB * 512 + 512],
                            start=True, stop=True),
                           "mm" if e3 == 2 else None)
                    W("scalar", "mm", val("mm"))
                    if b == 0:
                        W("scalar", "mm", e_hist)
                    op("scalar", lambda e: e.activation(tbuf[:], s_ps[:, 0:1536], AF.Tanh,
                                                        scale=float(SCALE / CAP)), "ac")
                    s_hist = val("ac")
                    op("scalar", lambda e, b=b: e.activation(ebuf[:, 3 * b:3 * b + 3, :], tbuf[:],
                                                             AF.Exp, scale=float(CAP)), "ac")
                W("vector", "ac", val("ac"))
                for j in range(4):
                    op("vector", lambda e, j=j: e.tensor_tensor(
                        ebuf[:, j, j * 128:j * 128 + 128], ebuf[:, j, j * 128:j * 128 + 128],
                        tri[:, 0:128], ALU.mult))
                    op("vector", lambda e, j=j: e.tensor_tensor(
                        ebuf[:, j + 8, j * 128:j * 128 + 128], ebuf[:, j + 8, j * 128:j * 128 + 128],
                        tri[:, 128:256], ALU.mult), "ev" if j == 3 else None)
                msk_done = val("ev")
                if dump and g == 0 and h == 0 and qB == 0:
                    W("sync", "ev", msk_done)
                    op("sync", lambda e: e.dma_start(out=dt_["d_e"], in_=ebuf[:]), "st", 16)
                    W("tensor", "st", val("st"))
                W("tensor", "vt", vt_grp)
                W("tensor", "ev", msk_done)
                for pair in range(2):
                    W("tensor", "ev", o_hist[pair])
                    for j in (2 * pair, 2 * pair + 1):
                        off = pair * 512 + (j % 2) * 256
                        for m in range(9):
                            kc = lq0 + j + m
                            op("tensor", lambda e, j=j, m=m, kc=kc, h=h, off=off: e.matmul(
                                o_ps[:, off:off + 129],
                                ebuf[:, j + m, j * 128:j * 128 + 128],
                                v_aug[:, kc, h * 144:h * 144 + 129],
                                start=(m == 0), stop=(m == 8)),
                               "mm" if (m == 8 and j % 2 == 1) else None)
                    m_pv = val("mm")
                    W("vector", "mm", m_pv)
                    for j in (2 * pair, 2 * pair + 1):
                        off = pair * 512 + (j % 2) * 256
                        op("vector", lambda e, j=j, off=off: e.reciprocal(
                            rcol[:, j], o_ps[:, off + 128:off + 129]), "vv" if j % 2 == 1 else None)
                    W("vector", "vv", val("vv"))
                    for j in (2 * pair, 2 * pair + 1):
                        off = pair * 512 + (j % 2) * 256
                        op("vector", lambda e, j=j, off=off: e.tensor_scalar(
                            o_nm[:, j, :], o_ps[:, off:off + 128], rcol[:, j], None, ALU.mult),
                           "ev" if j % 2 == 1 else None)
                    o_hist[pair] = val("ev")
                e_hist = val("mm")
                if dump and g == 0 and h == 0 and qB == 0:
                    W("sync", "ev", val("ev"))
                    op("sync", lambda e: e.dma_start(out=dt_["d_onm"], in_=o_nm[:]), "st", 16)
                    W("vector", "st", val("st"))
                W("tensor", "ev", val("ev"))
                W("tensor", "ev", t_hist)
                for j in range(4):
                    op("tensor", lambda e, j=j: e.transpose(
                        t_ps[:, j * 128:j * 128 + 128], o_nm[:, j, :], ident[:]),
                       "mm" if j == 3 else None)
                W("vector", "mm", val("mm"))
                op("vector", lambda e, ha=ha, qB=qB: e.tensor_copy(
                    o_fm[:, ha, qB * 512:qB * 512 + 512], t_ps[:]), "ev")
                t_hist = val("ev")

    pa_mm_done = val("mm")
    pa_ev_done = val("ev")
    if dump:
        W("sync", "ev", pa_ev_done)
        op("sync", lambda e: e.dma_start(out=dt_["d_ofm"], in_=o_fm[:]), "st", 16)

    def zview(tc, ob):
        return xn_fm[:, 2 * tc + ob // 2, (ob % 2) * 512:(ob % 2) * 512 + 512]

    wfb_hist = [pa_mm_done, pa_mm_done]
    silu_ac = [[0] * 4 for _ in range(8)]
    absw_pe = 0
    sgn_at = [0] * 4
    aev_at = [0] * 4

    def c1_load(ob):
        W("gpsimd", "mm", wfb_hist[ob % 2])
        op("gpsimd", lambda e, ob=ob: e.dma_start(out=wfb[:, ob % 2], in_=wf[ob]), "w", 16)

    def c1_alpha(ob):
        nonlocal absw_pe
        fb = ob % 2
        W("vector", "w", val("w"))
        if ob == 0:
            W("vector", "mm", pa_mm_done)
        W("vector", "mm", absw_pe)
        op("vector", lambda e, fb=fb: e.scalar_tensor_tensor(
            absw[:].rearrange("p a b -> p (a b)"), wfb[:, fb].rearrange("p a b -> p (a b)"),
            -1.0, wfb[:, fb].rearrange("p a b -> p (a b)"), ALU.mult, ALU.max), "ev")
        absw_done = val("ev")
        op("vector", lambda e, fb=fb: e.tensor_scalar(
            wfb[:, fb].rearrange("p a b -> p (a b)"), wfb[:, fb].rearrange("p a b -> p (a b)"),
            0.0, 0.5, ALU.is_ge, ALU.subtract), "ev")
        sgn_at[ob] = val("ev")
        W("tensor", "ev", absw_done)
        W("tensor", "ac", s_hist)
        for dk in range(16):
            op("tensor", lambda e, dk=dk: e.matmul(s_ps[0:1, 0:512], onesc[:], absw[:, dk, :],
                                                   start=(dk == 0), stop=(dk == 15)),
               "mm" if dk == 15 else None)
        absw_pe = val("mm")
        W("vector", "mm", val("mm"))
        op("vector", lambda e: e.tensor_copy(arow[:], s_ps[0:1, 0:512]), "ev")
        W("tensor", "ev", val("ev"))
        op("tensor", lambda e: e.matmul(s_ps[:, 512:1024], onesr[:], arow[:], start=True, stop=True), "mm")
        W("vector", "mm", val("mm"))
        op("vector", lambda e, fb=fb: e.tensor_copy(alpha_b[:, fb], s_ps[:, 512:1024]), "ev")
        aev_at[ob] = val("ev")

    def c1_ffn(ob):
        nonlocal prtile
        fb = ob % 2
        W("tensor", "ev", sgn_at[ob])
        for tc in range(8):
            pb = prtile % 2
            prtile += 1
            W("tensor", "ev", pr_hist[pb])
            for dk in range(16):
                op("tensor", lambda e, pb=pb, dk=dk, tc=tc, fb=fb: e.matmul(
                    pr_ps[:, pb * 512:pb * 512 + 512],
                    xn_fm[:, dk, 1024 + tc * 128:1024 + tc * 128 + 128],
                    wfb[:, fb, dk, :], start=(dk == 0), stop=(dk == 15)),
                   "mm" if dk == 15 else None)
            m = val("mm")
            W("vector", "mm", m)
            W("vector", "ev", aev_at[ob])
            op("vector", lambda e, pb=pb, fb=fb: e.scalar_tensor_tensor(
                tmp1[:], pr_ps[:, pb * 512:pb * 512 + 512], 1.0, alpha_b[:, fb], ALU.mult, ALU.mult))
            op("vector", lambda e, tc=tc, ob=ob: e.tensor_tensor(
                zview(tc, ob), tmp1[:], bias_sb[:, ob * 512:ob * 512 + 512], ALU.add), "ev")
            pr_hist[pb] = val("ev")
            W("scalar", "ev", val("ev"))
            op("scalar", lambda e, tc=tc, ob=ob: e.activation(
                zview(tc, ob), zview(tc, ob), AF.Silu), "ac")
            silu_ac[tc][ob] = val("ac")
        wfb_hist[fb] = val("mm")

    c1_load(0)
    c1_load(1)
    c1_alpha(0)
    c1_alpha(1)
    c1_ffn(0)
    c1_load(2)
    c1_ffn(1)
    c1_alpha(2)
    c1_load(3)
    c1_ffn(2)
    c1_alpha(3)
    c1_ffn(3)
    if dump:
        W("sync", "ev", aev_at[0] if False else val("ev"))
        op("sync", lambda e: e.dma_start(out=dt_["d_alpha"], in_=alpha_b[:, 1]), "st", 16)
    if dump:
        W("sync", "ac", silu_ac[7][3])
        op("sync", lambda e: e.dma_start(out=dt_["d_z"], in_=xn_fm[:, :, 0:1024]), "st", 16)
        W("scalar", "st", val("st"))

    xr_hist = [0, 0]
    st_hist = [0, 0]
    for half in range(2):
        for j2 in range(2):
            ob = half * 2 + j2
            W("gpsimd", "mm", wfb_hist[ob % 2])
            op("gpsimd", lambda e, ob=ob: e.dma_start(out=wfb[:, ob % 2], in_=wo[ob]), "w", 16)
        W("tensor", "w", val("w"))
        for tc in range(8):
            slot = tc % 2
            W("scalar", "ev", xr_hist[slot])
            op("scalar", lambda e, tc=tc, half=half, slot=slot: e.dma_start(
                out=xr[:, slot], in_=xs[1024 + tc * 128:1024 + tc * 128 + 128,
                                        half * 1024:half * 1024 + 1024]), "xr", 16)
            for j2 in range(2):
                ob = half * 2 + j2
                pb = prtile % 2
                prtile += 1
                W("tensor", "ev", pr_hist[pb])
                for dk in range(16):
                    op("tensor", lambda e, pb=pb, dk=dk, tc=tc, fb=ob % 2: e.matmul(
                        pr_ps[:, pb * 512:pb * 512 + 512],
                        o_fm[:, dk, tc * 128:tc * 128 + 128],
                        wfb[:, fb, dk, :], start=(dk == 0), stop=(dk == 15)),
                       "mm" if dk == 15 else None)
                m = val("mm")
                W("vector", "mm", m)
                W("vector", "ac", silu_ac[tc][ob])
                W("vector", "xr", val("xr"))
                W("vector", "st", st_hist[slot])
                op("vector", lambda e, pb=pb, tc=tc, ob=ob, j2=j2: e.scalar_tensor_tensor(
                    tmp_sg[:, j2], pr_ps[:, pb * 512:pb * 512 + 512], 1.0, zview(tc, ob),
                    ALU.mult, ALU.mult))
                op("vector", lambda e, slot=slot, j2=j2: e.tensor_tensor(
                    out_st[:, slot, j2 * 512:j2 * 512 + 512], tmp_sg[:, j2],
                    xr[:, slot, j2 * 512:j2 * 512 + 512], ALU.add), "ev")
                pr_hist[pb] = val("ev")
            xr_hist[slot] = val("ev")
            W("sync", "ev", val("ev"))
            op("sync", lambda e, tc=tc, half=half, slot=slot: e.dma_start(
                out=y[tc * 128:tc * 128 + 128, half * 1024:half * 1024 + 1024],
                in_=out_st[:, slot]), "st", 16)
            st_hist[slot] = val("st")
        wfb_hist[0] = val("mm")
        wfb_hist[1] = val("mm")

    W("sync", "st", val("st"))

    import contextlib
    with contextlib.ExitStack() as stack:
        sems = {}
        for name in cnt:
            sems[name] = stack.enter_context(nc.semaphore(name))
        nc._sems = sems
        with nc.Block() as blk:
            @blk.tensor
            def _(e):
                for f in plan["tensor"]:
                    f(e)

            @blk.vector
            def _(e):
                for f in plan["vector"]:
                    f(e)

            @blk.scalar
            def _(e):
                for f in plan["scalar"]:
                    f(e)

            @blk.gpsimd
            def _(e):
                for f in plan["gpsimd"]:
                    f(e)

            @blk.sync
            def _(e):
                for f in plan["sync"]:
                    f(e)
    return nc


def _prep_in_maps(L0_attn_x, L0_attn_Wq, L0_attn_Wk, L0_attn_Wv, L0_attn_Wo,
                  L0_ffn_weight, L0_ffn_bias):
    x = np.ascontiguousarray(np.asarray(L0_attn_x, dtype=np.float32))[0]

    def blk(w, ncols):
        wT = np.asarray(w, dtype=np.float32).T
        ng = 2048 // ncols
        return np.ascontiguousarray(wT.reshape(16, 128, ng, ncols).transpose(2, 1, 0, 3))

    wq_b = blk(L0_attn_Wq, 256)
    wk_b = blk(L0_attn_Wk, 256)
    wv_b = blk(L0_attn_Wv, 256)
    wo_b = blk(L0_attn_Wo, 512)
    wf_b = blk(L0_ffn_weight, 512)
    biasb = np.ascontiguousarray(np.broadcast_to(
        np.asarray(L0_ffn_bias, dtype=np.float32)[None, :], (128, 2048)))
    kk, qq = np.meshgrid(np.arange(128), np.arange(128), indexing="ij")
    tri = np.concatenate([(qq < kk), (qq >= kk)], axis=1).astype(ml_dtypes.bfloat16)
    ident = np.eye(128, dtype=ml_dtypes.bfloat16)
    onesc = np.full((128, 1), 2.0 / 2048.0, dtype=ml_dtypes.bfloat16)
    onesr = np.ones((1, 128), dtype=np.float32)

    in_maps = []
    for c in range(8):
        if c == 0:
            xsc = np.concatenate([np.zeros((1024, 2048), np.float32), x[0:1024]], axis=0)
            valid = np.array([0.0] * 8 + [1.0] * 8, np.float32)
        else:
            xsc = x[c * 1024 - 1024: c * 1024 + 1024]
            valid = np.ones(16, np.float32)
        xsc = np.ascontiguousarray(xsc)
        vc = np.ascontiguousarray(np.broadcast_to(valid[None, :, None], (128, 16, 1)).astype(np.float32))
        in_maps.append({
            "xs": xsc, "wq": wq_b, "wk": wk_b, "wv": wv_b, "wf": wf_b, "wo": wo_b,
            "biasb": biasb, "vcol": vc, "trih": tri, "identh": ident,
            "onesch": onesc, "onesrh": onesr,
        })
    return in_maps


def kernel(**inputs):
    in_maps = _prep_in_maps(**inputs)
    nc = build_nc()
    res = run_bass_kernel_spmd(nc, in_maps, core_ids=list(range(8)))
    out = np.concatenate([res.results[c]["y"] for c in range(8)], axis=0)
    return out[None, :, :].astype(np.float32)


if __name__ == "__main__":
    rng = np.random.RandomState(0)
    ins = {
        "L0_attn_x": rng.randn(1, 8192, 2048).astype(np.float32),
        "L0_attn_Wq": (rng.randn(2048, 2048) / 45.25).astype(np.float32),
        "L0_attn_Wk": (rng.randn(2048, 2048) / 45.25).astype(np.float32),
        "L0_attn_Wv": (rng.randn(2048, 2048) / 45.25).astype(np.float32),
        "L0_attn_Wo": (rng.randn(2048, 2048) / 45.25).astype(np.float32),
        "L0_ffn_weight": (rng.randn(2048, 2048) * 0.03125).astype(np.float32),
        "L0_ffn_bias": (rng.randn(2048) * 0.01).astype(np.float32),
    }
    out = kernel(**ins)
    print(out.shape, out.dtype, np.abs(out).mean())
